# version 35
# baseline (speedup 1.0000x reference)
import numpy as np
from contextlib import ExitStack

import concourse.bass as bass
import concourse.mybir as mybir
from concourse.bass_utils import run_bass_kernel_spmd

F32 = mybir.dt.float32
BF16 = mybir.dt.bfloat16
AF = mybir.ActivationFunctionType
OP = mybir.AluOpType

D = 2048
NCH = 16
QO = 48
FI = 44
TT = 512
NTILE = 8
TCORE = TT * NTILE
SCALE_O = float(2.0 ** -3.5)
LN_EPS = 1e-5
ATT_EPS = 1e-9


def build_nc():
    nc = bass.Bass("TRN2", target_bir_lowering=False)
    x_h = nc.dram_tensor("x", [D, TCORE], F32, kind="ExternalInput").ap()
    wqkv_h = nc.dram_tensor("wqkv", [QO, 128, 2048], F32, kind="ExternalInput").ap()
    wout_h = nc.dram_tensor("wout", [NCH, 128, 2048], F32, kind="ExternalInput").ap()
    wgate_h = nc.dram_tensor("wgate", [FI, 128, 2048], F32, kind="ExternalInput").ap()
    wup_h = nc.dram_tensor("wup", [FI, 128, 2048], F32, kind="ExternalInput").ap()
    wdown_h = nc.dram_tensor("wdown", [NCH, 128, FI * 128], F32, kind="ExternalInput").ap()
    gs_h = nc.dram_tensor("gs", [128, FI], F32, kind="ExternalInput").ap()
    us_h = nc.dram_tensor("us", [128, FI], F32, kind="ExternalInput").ap()
    ds_h = nc.dram_tensor("ds", [128, NCH], F32, kind="ExternalInput").ap()
    out_h = nc.dram_tensor("out", [D, TCORE], F32, kind="ExternalOutput").ap()
    cqkv_h = nc.dram_tensor("cqkv", [QO, 128, 2048], BF16, kind="Internal").ap()
    cwout_h = nc.dram_tensor("cwout", [NCH, 128, 2048], BF16, kind="Internal").ap()
    cgate_h = nc.dram_tensor("cgate", [FI, 128, 2048], BF16, kind="Internal").ap()
    cup_h = nc.dram_tensor("cup", [FI, 128, 2048], BF16, kind="Internal").ap()
    cdown_h = nc.dram_tensor("cdown", [NCH, 128, FI * 128], BF16, kind="Internal").ap()

    es = ExitStack()
    sb = lambda nm, shp, dt: es.enter_context(nc.sbuf_tensor(nm, shp, dt))
    with es:
        x_t = sb("x_t", [128, NCH * TT], F32)
        bufA = sb("bufA", [128, NCH * TT], BF16)
        bufB = sb("bufB", [128, NCH * TT], BF16)
        hdn_t = sb("hdn_t", [128, FI * TT], BF16)
        mean_b = sb("mean_b", [128, TT], BF16)
        rstd_b = sb("rstd_b", [128, TT], BF16)
        xscb = sb("xscb", [128, TT], BF16)
        xscb4 = [sb(f"xscb4_{i}", [128, TT], BF16) for i in range(4)]
        stg32 = [sb(f"stg32_{i}", [128, TT], F32) for i in range(2)]
        xsc = sb("xsc", [128, TT], F32)
        a1 = sb("a1", [128, TT], F32)
        sg_buf = [sb(f"sg{p}", [128, TT], F32) for p in range(2)]
        e_q = [sb(f"e_q{p}", [128, TT], F32) for p in range(2)]
        e_k = [sb(f"e_k{p}", [128, TT], F32) for p in range(2)]
        q_s = sb("q_s", [128, TT], F32)
        k_s = sb("k_s", [128, TT], F32)
        KV_b = sb("KV_b", [128, TT], F32)
        K_b = sb("K_b", [128, TT], F32)
        n_b = sb("n_b", [128, TT], F32)
        d_b = sb("d_b", [128, TT], F32)
        r_b = sb("r_b", [128, TT], F32)
        carKV = sb("carKV", [128, NCH], F32)
        carK = sb("carK", [128, NCH], F32)
        Wb = [sb(f"W{i}", [128, 2048], BF16) for i in range(6)]
        DB = [sb(f"DB{i}", [128, FI * 128], BF16) for i in range(2)]
        gs_s = sb("gs_s", [128, FI], F32)
        us_s = sb("us_s", [128, FI], F32)
        ds_s = sb("ds_s", [128, NCH], F32)
        ones_row = sb("ones_row", [1, 128], F32)
        ones_col = sb("ones_col", [128, 1], BF16)
        rows_a = sb("rows_a", [128, TT], F32)
        rows_b = sb("rows_b", [128, TT], F32)
        mean_row = rows_a[0:1, :]
        m2_row = rows_a[32:33, :]
        var_row = rows_a[64:65, :]
        lnv_row = rows_b[32:33, :]
        rstd_row = rows_b[0:1, :]
        ostg = [sb(f"ostg{i}", [128, TT], F32) for i in range(2)]
        eps5 = sb("eps5", [128, 1], F32)
        eps9 = sb("eps9", [128, 1], F32)
        pb = [es.enter_context(nc.psum_tensor(f"pb{i}", [128, TT], F32)) for i in range(8)]
        sem_names = ["dW", "dW2", "dC", "dWB", "dX", "dP", "dS",
                     "sT", "sS", "sV", "sG"]
        sems = {nm: es.enter_context(nc.semaphore(nm)) for nm in sem_names}
        cnt = {nm: 0 for nm in sem_names}
        blk = es.enter_context(nc.Block())

        ops = {e: [] for e in ("tensor", "scalar", "vector", "gpsimd", "sync")}

        def emit(eng, fn, waits=(), inc=None, k=1, self_inc=False):
            ops[eng].append((tuple(waits), fn, None if self_inc else inc, k))
            if inc is not None:
                cnt[inc] += k
                return cnt[inc]
            return None

        W = lambda nm: (nm, cnt[nm])

        ev = {}

        def rec(name, sem, val):
            ev[name] = (sem, val)

        def wt(name):
            return ev[name] if name in ev else None

        def mk_waits(*names):
            out = []
            for nm in names:
                if nm is None:
                    continue
                if isinstance(nm, tuple):
                    out.append(nm)
                elif nm in ev:
                    out.append(ev[nm])
            return out

        def _init_consts(v):
            v.memset(ones_row[0:1, :], 1.0)
            v.memset(ones_col[:, 0:1], 1.0)
            v.memset(eps5[:, 0:1], LN_EPS)
            v.memset(eps9[:, 0:1], ATT_EPS)
            return v.memset(carKV[:, :], 0.0)
        rec("init_consts", "sV", emit("vector", _init_consts, inc="sV"))
        rec("init_carK", "sV", emit("vector", lambda v: v.memset(carK[:, :], 0.0), inc="sV"))

        def _load_scales(g):
            g.dma_start(gs_s[:], gs_h).then_inc(sems["dX"], 16)
            g.dma_start(us_s[:], us_h).then_inc(sems["dX"], 16)
            g.dma_start(ds_s[:], ds_h).then_inc(sems["dX"], 16)
        rec("scales", "dX", emit("sync", _load_scales, inc="dX", k=48, self_inc=True))

        for c in range(NCH):
            val = emit("sync",
                       (lambda c: lambda g: g.dma_start(
                           x_t[:, c * TT:(c + 1) * TT], x_h[c * 128:(c + 1) * 128, 0:TT]))(c),
                       inc="dX", k=16)
            rec(f"x_0_{c}", "dX", val)
        rec("xtile_0", "dX", cnt["dX"])

        def _cache(dst, srcap, lo, hi):
            return lambda g: g.dma_start(dst[lo:hi], srcap[lo:hi])

        for t in range(NTILE):
            t0 = t * TT
            if t == 0:
                for j in range(NCH):
                    w1 = mk_waits("xtile_0") if j == 0 else []
                    def _xbj(s, j=j):
                        return s.activation(bufB[:, j * TT:(j + 1) * TT],
                                            x_t[:, j * TT:(j + 1) * TT], AF.Copy)
                    emit("scalar", _xbj, waits=w1)
                for j in range(NCH):
                    def _xsqj(s, j=j):
                        return s.activation(bufA[:, j * TT:(j + 1) * TT],
                                            bufB[:, j * TT:(j + 1) * TT], AF.Square)
                    val = emit("scalar", _xsqj, inc="sS")
                rec(f"xsq_{t}", "sS", val)
                stats_waits = mk_waits(f"xsq_{t}", "init_consts")
                rec(f"stats_{t}", "sT", emit(
                    "tensor", (lambda bA, bB: lambda tn: (
                        [tn.matmul(pb[6][0:1, :], ones_col[:, 0:1],
                                   bB[:, j * TT:(j + 1) * TT],
                                   start=(j == 0), stop=(j == NCH - 1))
                         for j in range(NCH)],
                        [tn.matmul(pb[7][0:1, :], ones_col[:, 0:1],
                                   bA[:, j * TT:(j + 1) * TT],
                                   start=(j == 0), stop=(j == NCH - 1))
                         for j in range(NCH)],
                    )[1][-1])(bufA, bufB),
                    waits=stats_waits, inc="sT"))
            if t == 0:
                def _rowmath(v):
                    v.tensor_scalar_mul(mean_row, pb[6][0:1, :], 1.0 / D)
                    v.tensor_mul(m2_row, mean_row, mean_row)
                    return v.scalar_tensor_tensor(
                        var_row, pb[7][0:1, :], 1.0 / D, m2_row,
                        OP.mult, OP.subtract)
                rec(f"rowmath_{t}", "sV", emit("vector", _rowmath,
                                               waits=mk_waits(f"stats_{t}"), inc="sV"))
                def _rstd(s):
                    s.activation(lnv_row, var_row, AF.Ln, bias=eps5[0:1, 0:1])
                    return s.activation(rstd_row, lnv_row, AF.Exp, scale=-0.5)
                rec(f"rstd_{t}", "sS", emit("scalar", _rstd,
                                            waits=mk_waits(f"rowmath_{t}"), inc="sS"))
                def _bcast(tn):
                    tn.matmul(pb[6][:, :], ones_row[0:1, :], mean_row,
                              start=True, stop=True)
                    return tn.matmul(pb[7][:, :], ones_row[0:1, :], rstd_row,
                                     start=True, stop=True)
                rec(f"bcast_{t}", "sT", emit("tensor", _bcast,
                                             waits=mk_waits(f"rstd_{t}"), inc="sT"))
                def _bcevac(v):
                    v.tensor_copy(mean_b[:, :], pb[6][:, :])
                    return v.tensor_copy(rstd_b[:, :], pb[7][:, :])
                rec(f"bcevac_{t}", "sV", emit("vector", _bcevac,
                                              waits=mk_waits(f"bcast_{t}"), inc="sV"))
                def _a5h(lo, hi):
                    def f(v):
                        last = None
                        for j in range(lo, hi):
                            v.tensor_sub(xscb[:, :], bufB[:, j * TT:(j + 1) * TT],
                                         mean_b[:, :])
                            last = v.tensor_mul(bufB[:, j * TT:(j + 1) * TT],
                                                xscb[:, :], rstd_b[:, :])
                        return last
                    return f
                rec(f"a5h_{t}", "sV", emit("vector", _a5h(0, 8),
                                           waits=mk_waits(f"stats_{t}"), inc="sV"))
                rec(f"a5_{t}", "sV", emit("vector", _a5h(8, NCH), inc="sV"))

            if t == 4:
                def _creset(v):
                    v.memset(carKV[:, :], 0.0)
                    return v.memset(carK[:, :], 0.0)
                rec("creset4", "sV", emit("vector", _creset, inc="sV"))

            for c in range(NCH):
                p = c % 2
                bq, bk, bv = pb[3 * p], pb[3 * p + 1], pb[3 * p + 2]
                wq, wk, wv = Wb[3 * p], Wb[3 * p + 1], Wb[3 * p + 2]
                if t == 0 and c < 2:
                    war = []
                elif c < 2:
                    war = mk_waits(f"umm_{t-1}_43" if c == 0 else f"amm_{t-1}_15")
                else:
                    war = mk_waits(f"vmm_{t}_{c-2}")
                if t == 0:
                    if c >= 2:
                        war += mk_waits(f"wbq_{c-2}")
                    def _wdma(wq, wk, wv, c):
                        def f(g):
                            g.dma_start(wq[:, :], wqkv_h[c]).then_inc(sems["dW"], 16)
                            g.dma_start(wk[:, :], wqkv_h[16 + c]).then_inc(sems["dW"], 16)
                            g.dma_start(wv[:, :], wqkv_h[32 + c]).then_inc(sems["dW"], 16)
                        return f
                    rec(f"wqkv_{t}_{c}", "dW",
                        emit("gpsimd", _wdma(wq, wk, wv, c), waits=war, inc="dW", k=48,
                             self_inc=True))
                    if c >= 1:
                        cc = c - 1
                        qp, kp, vp = Wb[3 * (cc % 2)], Wb[3 * (cc % 2) + 1], Wb[3 * (cc % 2) + 2]
                        def _wbk(qp, kp, vp, cc):
                            def f(g):
                                g.dma_start(cqkv_h[cc], qp[:, :]).then_inc(sems["dWB"], 16)
                                g.dma_start(cqkv_h[16 + cc], kp[:, :]).then_inc(sems["dWB"], 16)
                                g.dma_start(cqkv_h[32 + cc], vp[:, :]).then_inc(sems["dWB"], 16)
                            return f
                        rec(f"wbq_{cc}", "dWB",
                            emit("gpsimd", _wbk(qp, kp, vp, cc),
                                 waits=mk_waits(f"vmm_{t}_{cc}"), inc="dWB", k=48,
                                 self_inc=True))
                elif c >= 2:
                    def _wdma(wq, wk, wv, c):
                        def f(g):
                            g.dma_start(wq[:, :], cqkv_h[c]).then_inc(sems["dW2"], 16)
                            g.dma_start(wk[:, :], cqkv_h[16 + c]).then_inc(sems["dW2"], 16)
                            g.dma_start(wv[:, :], cqkv_h[32 + c]).then_inc(sems["dW2"], 16)
                        return f
                    rec(f"wqkv_{t}_{c}", "dW2",
                        emit("sync", _wdma(wq, wk, wv, c), waits=war, inc="dW2", k=48,
                             self_inc=True))
                mmw = mk_waits(f"wqkv_{t}_{c}")
                if c < 2:
                    mmw += mk_waits(f"d3_{t-1}_15")
                else:
                    mmw += mk_waits(f"kv_{t}_{c-2}")
                def _mmg(bank, slab, c=c):
                    def f(tn):
                        last = None
                        for j in range(NCH):
                            last = tn.matmul(bank[:, :], slab[:, j * 128:(j + 1) * 128],
                                             bufB[:, j * TT:(j + 1) * TT],
                                             start=(j == 0), stop=(j == NCH - 1))
                        return last
                    return f
                if c == 0:
                    emit("tensor", (lambda bq=bq, wq=wq: lambda tn: [
                        tn.matmul(bq[:, :], wq[:, j * 128:(j + 1) * 128],
                                  bufB[:, j * TT:(j + 1) * TT],
                                  start=(j == 0), stop=False) for j in range(8)][-1])(),
                         waits=mmw + mk_waits(f"a5h_{t}"))
                    rec(f"qmm_{t}_{c}", "sT", emit(
                        "tensor", (lambda bq=bq, wq=wq: lambda tn: [
                            tn.matmul(bq[:, :], wq[:, j * 128:(j + 1) * 128],
                                      bufB[:, j * TT:(j + 1) * TT],
                                      start=False, stop=(j == NCH - 1))
                            for j in range(8, NCH)][-1])(),
                        waits=mk_waits(f"a5_{t}"), inc="sT"))
                else:
                    rec(f"qmm_{t}_{c}", "sT", emit("tensor", _mmg(bq, wq), waits=mmw,
                                                   inc="sT"))
                rec(f"kmm_{t}_{c}", "sT", emit("tensor", _mmg(bk, wk), inc="sT"))
                rec(f"vmm_{t}_{c}", "sT", emit("tensor", _mmg(bv, wv), inc="sT"))
                if c >= 1:
                    def _rc(s, cc=c - 1):
                        s.activation(a1[:, :], d_b[:, :], AF.Ln, bias=eps9[:, 0:1])
                        return s.activation(r_b[:, :], a1[:, :], AF.Exp, scale=-1.0)
                    rec(f"r_{t}_{c-1}", "sS", emit("scalar", _rc,
                                                   waits=mk_waits(f"d_{t}_{c-1}"), inc="sS"))
                def _phiq(s, bq=bq, p=p):
                    s.activation(a1[:, :], bq[:, :], AF.Relu, scale=-1.0)
                    return s.activation(e_q[p][:, :], a1[:, :], AF.Exp, scale=-1.0)
                rec(f"eq_{t}_{c}", "sS", emit("scalar", _phiq,
                                              waits=mk_waits(f"qmm_{t}_{c}"), inc="sS"))
                def _phik(s, bk=bk, p=p):
                    s.activation(a1[:, :], bk[:, :], AF.Relu, scale=-1.0)
                    return s.activation(e_k[p][:, :], a1[:, :], AF.Exp, scale=-1.0)
                rec(f"ek_{t}_{c}", "sS", emit("scalar", _phik,
                                              waits=mk_waits(f"kmm_{t}_{c}"), inc="sS"))
                if c >= 1:
                    cc = c - 1
                    def _ap(v, cc=cc):
                        return v.scalar_tensor_tensor(
                            bufA[:, cc * TT:(cc + 1) * TT], n_b[:, :], SCALE_O, r_b[:, :],
                            OP.mult, OP.mult)
                    rec(f"ap_{t}_{cc}", "sV", emit("vector", _ap,
                                                   waits=mk_waits(f"r_{t}_{cc}"), inc="sV"))
                def _qs(v, bq=bq, p=p):
                    return v.scalar_tensor_tensor(q_s[:, :], bq[:, :], 0.0, e_q[p][:, :],
                                                  OP.max, OP.add)
                emit("vector", _qs, waits=mk_waits(f"eq_{t}_{c}"))
                def _ks(v, bk=bk, p=p):
                    return v.scalar_tensor_tensor(k_s[:, :], bk[:, :], 0.0, e_k[p][:, :],
                                                  OP.max, OP.add)
                emit("vector", _ks, waits=mk_waits(f"ek_{t}_{c}"))
                kvw = mk_waits(f"vmm_{t}_{c}")
                if t == 0 and c == 0:
                    kvw += mk_waits("init_carK")
                if t == 4 and c == 0:
                    kvw += mk_waits("creset4")
                rec(f"kv_{t}_{c}", "sV", emit(
                    "vector", (lambda bv: lambda v: v.tensor_tensor(
                        n_b[:, :], k_s[:, :], bv[:, :], OP.mult))(bv),
                    waits=kvw, inc="sV"))
                def _scans(v, c=c):
                    v.tensor_tensor_scan(KV_b[:, :], n_b[:, :], n_b[:, :],
                                         carKV[:, c:c + 1], OP.add, OP.bypass)
                    return v.tensor_tensor_scan(K_b[:, :], k_s[:, :], k_s[:, :],
                                                carK[:, c:c + 1], OP.add, OP.bypass)
                emit("vector", _scans)
                emit("vector", lambda v: v.tensor_mul(n_b[:, :], q_s[:, :], KV_b[:, :]))
                rec(f"d_{t}_{c}", "sV", emit(
                    "vector", lambda v: v.tensor_mul(d_b[:, :], q_s[:, :], K_b[:, :]),
                    inc="sV"))
                if t not in (3, 7):
                    def _csave(v, c=c):
                        v.tensor_copy(carKV[:, c:c + 1], KV_b[:, TT - 1:TT])
                        return v.tensor_copy(carK[:, c:c + 1], K_b[:, TT - 1:TT])
                    emit("vector", _csave)
            if t == 0:
                def _wbk15(g):
                    g.dma_start(cqkv_h[15], Wb[3][:, :]).then_inc(sems["dWB"], 16)
                    g.dma_start(cqkv_h[31], Wb[4][:, :]).then_inc(sems["dWB"], 16)
                    g.dma_start(cqkv_h[47], Wb[5][:, :]).then_inc(sems["dWB"], 16)
                rec("wbq_15", "dWB", emit("gpsimd", _wbk15,
                                          waits=mk_waits("vmm_0_15"), inc="dWB", k=48,
                                          self_inc=True))
                rec("cacheQ", "sG", emit("gpsimd", lambda g: g.nop(),
                                         waits=[("dWB", cnt["dWB"])], inc="sG"))
            if 1 <= t <= 4:
                seg = [(cgate_h, wgate_h, p) for p in range(11)] +                       [(cup_h, wup_h, p) for p in range(11)]
                lo = (t - 1) * 6
                for sidx in range(lo, min(lo + 6, 22)):
                    dsth, srch, cp = seg[sidx]
                    emit("gpsimd", _cache(dsth, srch, 4 * cp, 4 * cp + 4),
                         waits=mk_waits(f"kv_{t}_{min(13, 2 * (sidx - lo) + 1)}"),
                         inc="dC", k=16)
                if t < 4:
                    emit("gpsimd", lambda g: g.nop(), waits=[("dC", cnt["dC"])])
                else:
                    rec("cacheGU2", "sG", emit("gpsimd", lambda g: g.nop(),
                                               waits=[("dC", cnt["dC"])], inc="sG"))
            def _rc15(s):
                s.activation(a1[:, :], d_b[:, :], AF.Ln, bias=eps9[:, 0:1])
                return s.activation(r_b[:, :], a1[:, :], AF.Exp, scale=-1.0)
            rec(f"r_{t}_15", "sS", emit("scalar", _rc15,
                                        waits=mk_waits(f"d_{t}_15"), inc="sS"))
            def _ap15(v):
                return v.scalar_tensor_tensor(
                    bufA[:, 15 * TT:16 * TT], n_b[:, :], SCALE_O, r_b[:, :],
                    OP.mult, OP.mult)
            rec(f"ap_{t}_15", "sV", emit("vector", _ap15,
                                         waits=mk_waits(f"r_{t}_15"), inc="sV"))

            for i in range(FI):
                p = i % 2
                bg, bu = pb[2 * p], pb[2 * p + 1]
                wg, wu = Wb[2 * p], Wb[2 * p + 1]
                if i == 0:
                    war = mk_waits(f"vmm_{t}_14")
                elif i == 1:
                    war = mk_waits(f"vmm_{t}_15")
                else:
                    war = mk_waits(f"umm_{t}_{i-2}")
                if t <= 4:
                    if t == 0 and i == 0:
                        war += mk_waits("wbq_14")
                    elif t == 0 and i == 1:
                        war += mk_waits("wbq_15")
                    def _wdma(wg, wu, i):
                        def f(g):
                            g.dma_start(wg[:, :], wgate_h[i]).then_inc(sems["dW"], 16)
                            g.dma_start(wu[:, :], wup_h[i]).then_inc(sems["dW"], 16)
                        return f
                    rec(f"wgu_{t}_{i}", "dW", emit("gpsimd", _wdma(wg, wu, i),
                                                   waits=war, inc="dW", k=32,
                                                   self_inc=True))
                else:
                    if t == 5 and i == 0:
                        war += mk_waits("cacheGU2")
                    def _wdma(wg, wu, i):
                        def f(g):
                            g.dma_start(wg[:, :], cgate_h[i]).then_inc(sems["dW2"], 16)
                            g.dma_start(wu[:, :], cup_h[i]).then_inc(sems["dW2"], 16)
                        return f
                    rec(f"wgu_{t}_{i}", "dW2", emit("sync", _wdma(wg, wu, i),
                                                    waits=war, inc="dW2", k=32,
                                                    self_inc=True))
                if i == 0:
                    pw = mk_waits(f"kv_{t}_14")
                elif i == 1:
                    pw = mk_waits(f"kv_{t}_15")
                else:
                    pw = mk_waits(f"hdn_{t}_{i-2}")
                def _mmg(bank, slab):
                    def f(tn):
                        last = None
                        for j in range(NCH):
                            last = tn.matmul(bank[:, :], slab[:, j * 128:(j + 1) * 128],
                                             bufB[:, j * TT:(j + 1) * TT],
                                             start=(j == 0), stop=(j == NCH - 1))
                        return last
                    return f
                rec(f"gmm_{t}_{i}", "sT", emit(
                    "tensor", _mmg(bg, wg), waits=mk_waits(f"wgu_{t}_{i}") + pw, inc="sT"))
                rec(f"umm_{t}_{i}", "sT", emit("tensor", _mmg(bu, wu), inc="sT"))
                sw = mk_waits(f"gmm_{t}_{i}")
                if t == 0 and i == 0:
                    sw += mk_waits("xtile_0")
                def _silu(s, bg=bg, i=i, p=p):
                    return s.activation(sg_buf[p][:, :], bg[:, :], AF.Silu,
                                        scale=gs_s[:, i:i + 1])
                rec(f"silu_{t}_{i}", "sS", emit("scalar", _silu, waits=sw, inc="sS"))
                def _hdn(v, bu=bu, i=i, p=p):
                    return v.scalar_tensor_tensor(
                        hdn_t[:, i * TT:(i + 1) * TT], bu[:, :], us_s[:, i:i + 1],
                        sg_buf[p][:, :], OP.mult, OP.mult)
                rec(f"hdn_{t}_{i}", "sV", emit(
                    "vector", _hdn,
                    waits=mk_waits(f"silu_{t}_{i}", f"umm_{t}_{i}"), inc="sV"))
                if t + 1 < NTILE and i >= 4 and i % 2 == 0 and (i - 4) // 2 < NCH:
                    j = (i - 4) // 2
                    spw = mk_waits(f"pcast_{t+1}_{j}")
                    if j == 0:
                        spw += mk_waits(f"bcevac_{t}")
                    def _sp(tn, j=j):
                        tn.matmul(pb[6][0:1, :], ones_col[:, 0:1],
                                  xscb4[j % 2][:, :],
                                  start=(j == 0), stop=(j == NCH - 1),
                                  skip_group_check=True)
                        return tn.matmul(pb[7][0:1, :], ones_col[:, 0:1],
                                         xscb4[2 + j % 2][:, :],
                                         start=(j == 0), stop=(j == NCH - 1),
                                         skip_group_check=True)
                    rec(f"sp_{t+1}_{j}", "sT", emit("tensor", _sp, waits=spw,
                                                    inc="sT"))
                if t + 1 < NTILE and i % 2 == 0 and i // 2 < NCH:
                    j = i // 2
                    dw = mk_waits(f"pcast_{t+1}_{j-2}") if j >= 2 else []
                    rec(f"px_{t+1}_{j}", "dP", emit(
                        "sync", (lambda j=j, t1=t + 1: lambda g: g.dma_start(
                            stg32[j % 2][:, :],
                            x_h[j * 128:(j + 1) * 128, t1 * TT:(t1 + 1) * TT]))(),
                        waits=dw, inc="dP", k=16))
                    cw = mk_waits(f"px_{t+1}_{j}")
                    if j >= 2:
                        cw += mk_waits(f"sp_{t+1}_{j-2}")
                    def _pc(s, j=j):
                        s.activation(xscb4[j % 2][:, :], stg32[j % 2][:, :], AF.Copy)
                        return s.activation(xscb4[2 + j % 2][:, :],
                                            xscb4[j % 2][:, :], AF.Square)
                    rec(f"pcast_{t+1}_{j}", "sS", emit("scalar", _pc, waits=cw,
                                                       inc="sS"))

            if t + 1 < NTILE:
                def _rowm1(v):
                    v.tensor_scalar_mul(mean_row, pb[6][0:1, :], 1.0 / D)
                    v.tensor_mul(m2_row, mean_row, mean_row)
                    return v.scalar_tensor_tensor(
                        var_row, pb[7][0:1, :], 1.0 / D, m2_row,
                        OP.mult, OP.subtract)
                rec(f"rowmath_{t+1}", "sV", emit(
                    "vector", _rowm1, waits=mk_waits(f"sp_{t+1}_{NCH-1}"), inc="sV"))
                def _rstd1(s):
                    s.activation(lnv_row, var_row, AF.Ln, bias=eps5[0:1, 0:1])
                    return s.activation(rstd_row, lnv_row, AF.Exp, scale=-0.5)
                rec(f"rstd_{t+1}", "sS", emit("scalar", _rstd1,
                                              waits=mk_waits(f"rowmath_{t+1}"),
                                              inc="sS"))
                rec("cacheGU", "sG", emit("gpsimd", lambda g: g.nop(),
                                          waits=[("dWB", cnt["dWB"])], inc="sG"))
            if t + 1 < NTILE:
                def _bc1(tn):
                    tn.matmul(pb[6][:, :], ones_row[0:1, :], mean_row,
                              start=True, stop=True)
                    return tn.matmul(pb[7][:, :], ones_row[0:1, :], rstd_row,
                                     start=True, stop=True)
                rec(f"bcast_{t+1}", "sT", emit("tensor", _bc1,
                                               waits=mk_waits(f"rstd_{t+1}"),
                                               inc="sT"))
                def _bce1(v):
                    v.tensor_copy(mean_b[:, :], pb[6][:, :])
                    return v.tensor_copy(rstd_b[:, :], pb[7][:, :])
                rec(f"bcevac_{t+1}", "sV", emit("vector", _bce1,
                                                waits=mk_waits(f"bcast_{t+1}"),
                                                inc="sV"))
            if t + 1 < NTILE:
                for jj in (14, 15):
                    rec(f"px2_{t+1}_{jj}", "dP", emit(
                        "sync", (lambda jj=jj, t1=t + 1: lambda g: g.dma_start(
                            stg32[jj - 14][:, :],
                            x_h[jj * 128:(jj + 1) * 128, t1 * TT:(t1 + 1) * TT]))(),
                        waits=mk_waits(f"pcast_{t+1}_{jj}"), inc="dP", k=16))
            for o in range(NCH):
                p = o % 2
                ba, bf = pb[p], pb[4 + p]
                wo = Wb[4 + p]
                dbuf = DB[p]
                if o < 2:
                    war = mk_waits(f"vmm_{t}_15")
                else:
                    war = mk_waits(f"amm_{t}_{o-2}")
                if t == 0 and o < 2:
                    war += mk_waits("wbq_15")
                def _wodma(wo, o):
                    return lambda g: g.dma_start(wo[:, :], wout_h[o])
                rec(f"wo_{t}_{o}", "dW", emit("gpsimd", _wodma(wo, o), waits=war,
                                              inc="dW", k=16))
                dwar = [] if t == 0 and o < 2 else mk_waits(
                    f"fmm_{t}_{o-2}" if o >= 2 else f"fmm_{t-1}_{14+o}")
                def _wddma(dbuf, o):
                    return lambda g: g.dma_start(dbuf[:, :], wdown_h[o])
                rec(f"wd_{t}_{o}", "dW", emit("gpsimd", _wddma(dbuf, o), waits=dwar,
                                              inc="dW", k=16))
                mmw = mk_waits(f"wo_{t}_{o}", f"ap_{t}_15")
                if o < 2:
                    mmw += mk_waits(f"hdn_{t}_42" if o == 0 else f"hdn_{t}_43")
                else:
                    mmw += mk_waits(f"d3_{t}_{o-2}")
                def _amm(ba, wo):
                    def f(tn):
                        last = None
                        for j in range(NCH):
                            last = tn.matmul(ba[:, :], wo[:, j * 128:(j + 1) * 128],
                                             bufA[:, j * TT:(j + 1) * TT],
                                             start=(j == 0), stop=(j == NCH - 1))
                        return last
                    return f
                rec(f"amm_{t}_{o}", "sT", emit("tensor", _amm(ba, wo), waits=mmw, inc="sT"))
                fw = mk_waits(f"wd_{t}_{o}", f"hdn_{t}_43")
                def _fmm(bf, dbuf):
                    def f(tn):
                        last = None
                        for i in range(FI):
                            last = tn.matmul(bf[:, :], dbuf[:, i * 128:(i + 1) * 128],
                                             hdn_t[:, i * TT:(i + 1) * TT],
                                             start=(i == 0), stop=(i == FI - 1))
                        return last
                    return f
                rec(f"fmm_{t}_{o}", "sT", emit("tensor", _fmm(bf, dbuf), waits=fw, inc="sT"))
                ob = ostg[o % 2]
                ocol = 0
                vw = mk_waits(f"fmm_{t}_{o}")
                sidx = 16 * t + o - 2 if o >= 2 else (16 * (t - 1) + o + 14 if t >= 1 else -1)
                if sidx >= 0:
                    vw += [("dS", 16 * (sidx + 1))]
                if t == 0 and o == 0:
                    vw += mk_waits("xtile_0")
                def _stt(v, bf=bf, o=o):
                    return v.scalar_tensor_tensor(
                        xsc[:, :], bf[:, :], ds_s[:, o:o + 1], x_t[:, o * TT:(o + 1) * TT],
                        OP.mult, OP.add)
                emit("vector", _stt, waits=vw)
                def _fin(v, ba=ba, ob=ob, ocol=ocol):
                    return v.tensor_add(ob[:, ocol:ocol + TT], xsc[:, :], ba[:, :])
                rec(f"d3_{t}_{o}", "sV", emit("vector", _fin, inc="sV"))
                if t + 1 < NTILE:
                    if o >= 14:
                        xw = mk_waits(f"px2_{t+1}_{o}")
                        srcap = stg32[o - 14][:, :]
                    else:
                        xw = [("dX", 48 + 16 * (16 * (t + 1) + o + 1))]
                        srcap = x_t[:, o * TT:(o + 1) * TT]
                    if o == 0:
                        xw += mk_waits(f"umm_{t}_43")
                    def _xbj(s, o=o, srcap=srcap):
                        return s.activation(bufB[:, o * TT:(o + 1) * TT],
                                            srcap, AF.Copy)
                    rec(f"xbc_{t+1}_{o}", "sS", emit("scalar", _xbj, waits=xw,
                                                     inc="sS"))
                    if o >= 2:
                        jj = o - 2
                        a5w = mk_waits(f"xbc_{t+1}_{jj}")
                        if jj == 0:
                            a5w += mk_waits(f"bcevac_{t+1}")
                        def _a5j(v, jj=jj):
                            v.tensor_sub(xscb[:, :], bufB[:, jj * TT:(jj + 1) * TT],
                                         mean_b[:, :])
                            return v.tensor_mul(bufB[:, jj * TT:(jj + 1) * TT],
                                                xscb[:, :], rstd_b[:, :])
                        val = emit("vector", _a5j, waits=a5w,
                                   inc="sV" if jj in (7, NCH - 1) else None)
                        if jj == 7:
                            rec(f"a5h_{t+1}", "sV", val)
            if t + 1 < NTILE:
                for jj in (14, 15):
                    def _a5j(v, jj=jj):
                        v.tensor_sub(xscb[:, :], bufB[:, jj * TT:(jj + 1) * TT],
                                     mean_b[:, :])
                        return v.tensor_mul(bufB[:, jj * TT:(jj + 1) * TT],
                                            xscb[:, :], rstd_b[:, :])
                    val = emit("vector", _a5j, waits=mk_waits(f"xbc_{t+1}_{jj}"),
                               inc="sV" if jj == NCH - 1 else None)
                rec(f"a5_{t+1}", "sV", val)
            def _emit_pre_triple(c2, t=t):
                war2 = mk_waits(f"umm_{t}_43" if c2 == 0 else f"amm_{t}_15")
                if t == 0 and c2 == 0:
                    war2 += mk_waits("wbq_15")
                wq2, wk2, wv2 = Wb[3 * c2], Wb[3 * c2 + 1], Wb[3 * c2 + 2]
                def _wdma2(wq, wk, wv, c):
                    def f(g):
                        g.dma_start(wq[:, :], cqkv_h[c]).then_inc(sems["dW2"], 16)
                        g.dma_start(wk[:, :], cqkv_h[16 + c]).then_inc(sems["dW2"], 16)
                        g.dma_start(wv[:, :], cqkv_h[32 + c]).then_inc(sems["dW2"], 16)
                    return f
                rec(f"wqkv_{t+1}_{c2}", "dW2",
                    emit("sync", _wdma2(wq2, wk2, wv2, c2), waits=war2,
                         inc="dW2", k=48, self_inc=True))
            if t + 1 < NTILE:
                _emit_pre_triple(0)
            for o in range(NCH):
                if o == 14 and t + 1 < NTILE:
                    _emit_pre_triple(1)
                if t + 1 < NTILE:
                    def _xld(g_, o=o, t1=t + 1):
                        return g_.dma_start(x_t[:, o * TT:(o + 1) * TT],
                                            x_h[o * 128:(o + 1) * 128,
                                                t1 * TT:(t1 + 1) * TT])
                    rec(f"x_{t+1}_{o}", "dX", emit("sync", _xld,
                                                   waits=mk_waits(f"d3_{t}_{o}"),
                                                   inc="dX", k=16))
                ob2 = ostg[o % 2]
                def _st(s, t=t, o=o, ob2=ob2):
                    return s.dma_start(
                        out_h[o * 128:(o + 1) * 128, t * TT:(t + 1) * TT],
                        ob2[:, :])
                rec(f"store_{t}_{o}", "dS", emit(
                    "sync", _st, waits=mk_waits(f"d3_{t}_{o}"), inc="dS", k=16))
            if t + 1 < NTILE:
                rec(f"xtile_{t+1}", "dX", cnt["dX"])

        emit("sync", lambda s: s.nop(), waits=[("dS", cnt["dS"])])

        def runner(lst):
            def f(eng):
                for waits, fn, inc, k in lst:
                    for (snm, v) in waits:
                        eng.wait_ge(sems[snm], v)
                    ins = fn(eng)
                    if inc is not None:
                        ins.then_inc(sems[inc], k)
            return f

        blk.tensor(runner(ops["tensor"]))
        blk.scalar(runner(ops["scalar"]))
        blk.vector(runner(ops["vector"]))
        blk.gpsimd(runner(ops["gpsimd"]))
        blk.sync(runner(ops["sync"]))
    return nc


def _prep_inputs(L0_attn_x, L0_attn_W_qkv, L0_attn_W_out, L0_ffn_gate_w,
                 L0_ffn_up_w, L0_ffn_down_w, L0_ffn_gate_s, L0_ffn_up_s,
                 L0_ffn_down_s):
    f32 = np.float32
    wqkv = np.ascontiguousarray(
        np.asarray(L0_attn_W_qkv, f32).reshape(48, 128, 16, 128)
        .transpose(0, 3, 2, 1).reshape(48, 128, 2048))
    wout = np.ascontiguousarray(
        np.asarray(L0_attn_W_out, f32).reshape(16, 128, 16, 128)
        .transpose(0, 3, 2, 1).reshape(16, 128, 2048))
    wgate = np.ascontiguousarray(
        np.asarray(L0_ffn_gate_w, f32).reshape(44, 128, 16, 128)
        .transpose(0, 3, 2, 1).reshape(44, 128, 2048))
    wup = np.ascontiguousarray(
        np.asarray(L0_ffn_up_w, f32).reshape(44, 128, 16, 128)
        .transpose(0, 3, 2, 1).reshape(44, 128, 2048))
    wdown = np.ascontiguousarray(
        np.asarray(L0_ffn_down_w, f32).reshape(16, 128, 44, 128)
        .transpose(0, 3, 2, 1).reshape(16, 128, 5632))
    gs = np.ascontiguousarray(np.asarray(L0_ffn_gate_s, f32).reshape(44, 128).T)
    us = np.ascontiguousarray(np.asarray(L0_ffn_up_s, f32).reshape(44, 128).T)
    ds = np.ascontiguousarray(np.asarray(L0_ffn_down_s, f32).reshape(16, 128).T)
    x = np.asarray(L0_attn_x, f32)
    in_maps = []
    for c in range(8):
        xc = np.ascontiguousarray(x[2 * c:2 * c + 2].reshape(2 * 2048, 2048).T)
        in_maps.append(dict(x=xc, wqkv=wqkv, wout=wout, wgate=wgate, wup=wup,
                            wdown=wdown, gs=gs, us=us, ds=ds))
    return in_maps


def kernel(**inputs):
    in_maps = _prep_inputs(**inputs)
    nc = build_nc()
    res = run_bass_kernel_spmd(nc, in_maps, core_ids=list(range(8)))
    out = np.empty((16, 2048, 2048), dtype=np.float32)
    for c in range(8):
        oc = np.asarray(res.results[c]["out"])
        out[2 * c:2 * c + 2] = oc.T.reshape(2, 2048, 2048)
    return out


# version 37 (final)
# speedup vs baseline: 1.1306x; 1.1306x over previous
import numpy as np
from contextlib import ExitStack

import concourse.bass as bass
import concourse.mybir as mybir
from concourse.bass_utils import run_bass_kernel_spmd

F32 = mybir.dt.float32
BF16 = mybir.dt.bfloat16
AF = mybir.ActivationFunctionType
OP = mybir.AluOpType

D = 2048
NCH = 16
QO = 48
FI = 44
TT = 512
NTILE = 8
TCORE = TT * NTILE
SCALE_O = float(2.0 ** -3.5)
LN_EPS = 1e-5
ATT_EPS = 1e-9


def build_nc():
    nc = bass.Bass("TRN2", target_bir_lowering=False)
    x_h = nc.dram_tensor("x", [D, TCORE], F32, kind="ExternalInput").ap()
    wqkv_h = nc.dram_tensor("wqkv", [QO, 128, 2048], F32, kind="ExternalInput").ap()
    wout_h = nc.dram_tensor("wout", [NCH, 128, 2048], F32, kind="ExternalInput").ap()
    wgate_h = nc.dram_tensor("wgate", [FI, 128, 2048], F32, kind="ExternalInput").ap()
    wup_h = nc.dram_tensor("wup", [FI, 128, 2048], F32, kind="ExternalInput").ap()
    wdown_h = nc.dram_tensor("wdown", [NCH, 128, FI * 128], F32, kind="ExternalInput").ap()
    gs_h = nc.dram_tensor("gs", [128, FI], F32, kind="ExternalInput").ap()
    us_h = nc.dram_tensor("us", [128, FI], F32, kind="ExternalInput").ap()
    ds_h = nc.dram_tensor("ds", [128, NCH], F32, kind="ExternalInput").ap()
    out_h = nc.dram_tensor("out", [D, TCORE], F32, kind="ExternalOutput").ap()
    cqkv_h = nc.dram_tensor("cqkv", [QO, 128, 2048], BF16, kind="Internal").ap()
    cwout_h = nc.dram_tensor("cwout", [NCH, 128, 2048], BF16, kind="Internal").ap()
    cgate_h = nc.dram_tensor("cgate", [FI, 128, 2048], BF16, kind="Internal").ap()
    cup_h = nc.dram_tensor("cup", [FI, 128, 2048], BF16, kind="Internal").ap()
    cdown_h = nc.dram_tensor("cdown", [NCH, 128, FI * 128], BF16, kind="Internal").ap()

    es = ExitStack()
    sb = lambda nm, shp, dt: es.enter_context(nc.sbuf_tensor(nm, shp, dt))
    with es:
        x_t = sb("x_t", [128, NCH * TT], F32)
        bufA = sb("bufA", [128, NCH * TT], BF16)
        bufB = sb("bufB", [128, NCH * TT], BF16)
        hdn_t = sb("hdn_t", [128, FI * TT], BF16)
        mean_b = sb("mean_b", [128, TT], BF16)
        rstd_b = sb("rstd_b", [128, TT], BF16)
        xscb = sb("xscb", [128, TT], BF16)
        xscb4 = [sb(f"xscb4_{i}", [128, TT], BF16) for i in range(4)]
        stg32 = [sb(f"stg32_{i}", [128, TT], F32) for i in range(2)]
        xsc = sb("xsc", [128, TT], F32)
        a1 = sb("a1", [128, TT], F32)
        sg_buf = [sb(f"sg{p}", [128, TT], F32) for p in range(2)]
        e_q = [sb(f"e_q{p}", [128, TT], F32) for p in range(2)]
        e_k = [sb(f"e_k{p}", [128, TT], F32) for p in range(2)]
        q_s = sb("q_s", [128, TT], F32)
        k_s = sb("k_s", [128, TT], F32)
        KV_b = sb("KV_b", [128, TT], F32)
        K_b = sb("K_b", [128, TT], F32)
        n_b = sb("n_b", [128, TT], F32)
        d_b = sb("d_b", [128, TT], F32)
        r_b = sb("r_b", [128, TT], F32)
        carKV = sb("carKV", [128, NCH], F32)
        carK = sb("carK", [128, NCH], F32)
        Wb = [sb(f"W{i}", [128, 2048], BF16) for i in range(6)]
        DB = [sb(f"DB{i}", [128, FI * 128], BF16) for i in range(2)]
        gs_s = sb("gs_s", [128, FI], F32)
        us_s = sb("us_s", [128, FI], F32)
        ds_s = sb("ds_s", [128, NCH], F32)
        ones_row = sb("ones_row", [1, 128], F32)
        ones_col = sb("ones_col", [128, 1], BF16)
        rows_a = sb("rows_a", [128, TT], F32)
        rows_b = sb("rows_b", [128, TT], F32)
        mean_row = rows_a[0:1, :]
        m2_row = rows_a[32:33, :]
        var_row = rows_a[64:65, :]
        lnv_row = rows_b[32:33, :]
        rstd_row = rows_b[0:1, :]
        ostg = [sb(f"ostg{i}", [128, TT], F32) for i in range(2)]
        eps5 = sb("eps5", [128, 1], F32)
        eps9 = sb("eps9", [128, 1], F32)
        pb = [es.enter_context(nc.psum_tensor(f"pb{i}", [128, TT], F32)) for i in range(8)]
        sem_names = ["dW", "dW2", "dC", "dWB", "dX", "dP", "dS",
                     "sT", "sS", "sV", "sG"]
        sems = {nm: es.enter_context(nc.semaphore(nm)) for nm in sem_names}
        cnt = {nm: 0 for nm in sem_names}
        blk = es.enter_context(nc.Block())

        ops = {e: [] for e in ("tensor", "scalar", "vector", "gpsimd", "sync")}

        def emit(eng, fn, waits=(), inc=None, k=1, self_inc=False):
            ops[eng].append((tuple(waits), fn, None if self_inc else inc, k))
            if inc is not None:
                cnt[inc] += k
                return cnt[inc]
            return None

        W = lambda nm: (nm, cnt[nm])

        ev = {}

        def rec(name, sem, val):
            ev[name] = (sem, val)

        def wt(name):
            return ev[name] if name in ev else None

        def mk_waits(*names):
            out = []
            for nm in names:
                if nm is None:
                    continue
                if isinstance(nm, tuple):
                    out.append(nm)
                elif nm in ev:
                    out.append(ev[nm])
            return out

        def _init_consts(v):
            v.memset(ones_row[0:1, :], 1.0)
            v.memset(ones_col[:, 0:1], 1.0)
            v.memset(eps5[:, 0:1], LN_EPS)
            v.memset(eps9[:, 0:1], ATT_EPS)
            return v.memset(carKV[:, :], 0.0)
        rec("init_consts", "sV", emit("vector", _init_consts, inc="sV"))
        rec("init_carK", "sV", emit("vector", lambda v: v.memset(carK[:, :], 0.0), inc="sV"))

        def _load_scales(g):
            g.dma_start(gs_s[:], gs_h).then_inc(sems["dX"], 16)
            g.dma_start(us_s[:], us_h).then_inc(sems["dX"], 16)
            g.dma_start(ds_s[:], ds_h).then_inc(sems["dX"], 16)
        rec("scales", "dX", emit("sync", _load_scales, inc="dX", k=48, self_inc=True))

        for c in range(NCH):
            val = emit("sync",
                       (lambda c: lambda g: g.dma_start(
                           x_t[:, c * TT:(c + 1) * TT], x_h[c * 128:(c + 1) * 128, 0:TT]))(c),
                       inc="dX", k=16)
            rec(f"x_0_{c}", "dX", val)
        rec("xtile_0", "dX", cnt["dX"])

        def _cache(dst, srcap, lo, hi):
            return lambda g: g.dma_start(dst[lo:hi], srcap[lo:hi])

        for t in range(NTILE):
            t0 = t * TT
            if t == 0:
                for j in range(NCH):
                    w1 = mk_waits("xtile_0") if j == 0 else []
                    def _xbj(s, j=j):
                        return s.activation(bufB[:, j * TT:(j + 1) * TT],
                                            x_t[:, j * TT:(j + 1) * TT], AF.Copy)
                    emit("scalar", _xbj, waits=w1)
                for j in range(NCH):
                    def _xsqj(s, j=j):
                        return s.activation(bufA[:, j * TT:(j + 1) * TT],
                                            bufB[:, j * TT:(j + 1) * TT], AF.Square)
                    val = emit("scalar", _xsqj, inc="sS")
                rec(f"xsq_{t}", "sS", val)
                stats_waits = mk_waits(f"xsq_{t}", "init_consts")
                rec(f"stats_{t}", "sT", emit(
                    "tensor", (lambda bA, bB: lambda tn: (
                        [tn.matmul(pb[6][0:1, :], ones_col[:, 0:1],
                                   bB[:, j * TT:(j + 1) * TT],
                                   start=(j == 0), stop=(j == NCH - 1))
                         for j in range(NCH)],
                        [tn.matmul(pb[7][0:1, :], ones_col[:, 0:1],
                                   bA[:, j * TT:(j + 1) * TT],
                                   start=(j == 0), stop=(j == NCH - 1))
                         for j in range(NCH)],
                    )[1][-1])(bufA, bufB),
                    waits=stats_waits, inc="sT"))
            if t == 0:
                def _rowmath(v):
                    v.tensor_scalar_mul(mean_row, pb[6][0:1, :], 1.0 / D)
                    v.tensor_mul(m2_row, mean_row, mean_row)
                    return v.scalar_tensor_tensor(
                        var_row, pb[7][0:1, :], 1.0 / D, m2_row,
                        OP.mult, OP.subtract)
                rec(f"rowmath_{t}", "sV", emit("vector", _rowmath,
                                               waits=mk_waits(f"stats_{t}"), inc="sV"))
                def _rstd(s):
                    s.activation(lnv_row, var_row, AF.Ln, bias=eps5[0:1, 0:1])
                    return s.activation(rstd_row, lnv_row, AF.Exp, scale=-0.5)
                rec(f"rstd_{t}", "sS", emit("scalar", _rstd,
                                            waits=mk_waits(f"rowmath_{t}"), inc="sS"))
                def _bcast(tn):
                    tn.matmul(pb[6][:, :], ones_row[0:1, :], mean_row,
                              start=True, stop=True)
                    return tn.matmul(pb[7][:, :], ones_row[0:1, :], rstd_row,
                                     start=True, stop=True)
                rec(f"bcast_{t}", "sT", emit("tensor", _bcast,
                                             waits=mk_waits(f"rstd_{t}"), inc="sT"))
                def _bcevac(v):
                    v.tensor_copy(mean_b[:, :], pb[6][:, :])
                    return v.tensor_copy(rstd_b[:, :], pb[7][:, :])
                rec(f"bcevac_{t}", "sV", emit("vector", _bcevac,
                                              waits=mk_waits(f"bcast_{t}"), inc="sV"))
                def _a5h(lo, hi):
                    def f(v):
                        last = None
                        for j in range(lo, hi):
                            v.tensor_sub(xscb[:, :], bufB[:, j * TT:(j + 1) * TT],
                                         mean_b[:, :])
                            last = v.tensor_mul(bufB[:, j * TT:(j + 1) * TT],
                                                xscb[:, :], rstd_b[:, :])
                        return last
                    return f
                rec(f"a5h_{t}", "sV", emit("vector", _a5h(0, 8),
                                           waits=mk_waits(f"stats_{t}"), inc="sV"))
                rec(f"a5_{t}", "sV", emit("vector", _a5h(8, NCH), inc="sV"))

            if t == 4:
                def _creset(v):
                    v.memset(carKV[:, :], 0.0)
                    return v.memset(carK[:, :], 0.0)
                rec("creset4", "sV", emit("vector", _creset, inc="sV"))

            for c in range(NCH):
                p = c % 2
                bq, bk, bv = pb[3 * p], pb[3 * p + 1], pb[3 * p + 2]
                if c == 0:
                    bq, bk = pb[6], pb[7]
                wq, wk, wv = Wb[3 * p], Wb[3 * p + 1], Wb[3 * p + 2]
                if t == 0 and c < 2:
                    war = []
                elif c < 2:
                    war = mk_waits(f"umm_{t-1}_43" if c == 0 else f"amm_{t-1}_15")
                else:
                    war = mk_waits(f"vmm_{t}_{c-2}")
                if t == 0:
                    if c >= 2:
                        war += mk_waits(f"wbq_{c-2}")
                    def _wdma(wq, wk, wv, c):
                        def f(g):
                            g.dma_start(wq[:, :], wqkv_h[c]).then_inc(sems["dW"], 16)
                            g.dma_start(wk[:, :], wqkv_h[16 + c]).then_inc(sems["dW"], 16)
                            g.dma_start(wv[:, :], wqkv_h[32 + c]).then_inc(sems["dW"], 16)
                        return f
                    rec(f"wqkv_{t}_{c}", "dW",
                        emit("gpsimd", _wdma(wq, wk, wv, c), waits=war, inc="dW", k=48,
                             self_inc=True))
                    if c >= 1:
                        cc = c - 1
                        qp, kp, vp = Wb[3 * (cc % 2)], Wb[3 * (cc % 2) + 1], Wb[3 * (cc % 2) + 2]
                        def _wbk(qp, kp, vp, cc):
                            def f(g):
                                g.dma_start(cqkv_h[cc], qp[:, :]).then_inc(sems["dWB"], 16)
                                g.dma_start(cqkv_h[16 + cc], kp[:, :]).then_inc(sems["dWB"], 16)
                                g.dma_start(cqkv_h[32 + cc], vp[:, :]).then_inc(sems["dWB"], 16)
                            return f
                        rec(f"wbq_{cc}", "dWB",
                            emit("gpsimd", _wbk(qp, kp, vp, cc),
                                 waits=mk_waits(f"vmm_{t}_{cc}"), inc="dWB", k=48,
                                 self_inc=True))
                elif c >= 2:
                    def _wdma(wq, wk, wv, c):
                        def f(g):
                            g.dma_start(wq[:, :], cqkv_h[c]).then_inc(sems["dW2"], 16)
                            g.dma_start(wk[:, :], cqkv_h[16 + c]).then_inc(sems["dW2"], 16)
                            g.dma_start(wv[:, :], cqkv_h[32 + c]).then_inc(sems["dW2"], 16)
                        return f
                    rec(f"wqkv_{t}_{c}", "dW2",
                        emit("sync", _wdma(wq, wk, wv, c), waits=war, inc="dW2", k=48,
                             self_inc=True))
                mmw = mk_waits(f"wqkv_{t}_{c}")
                if c == 0:
                    mmw += mk_waits(f"silu_{t-1}_43")
                elif c == 1:
                    mmw += mk_waits(f"d3_{t-1}_15")
                else:
                    mmw += mk_waits(f"kv_{t}_{c-2}")
                    if c == 2:
                        mmw += mk_waits(f"d3_{t-1}_15")
                def _mmg(bank, slab, c=c):
                    def f(tn):
                        last = None
                        for j in range(NCH):
                            last = tn.matmul(bank[:, :], slab[:, j * 128:(j + 1) * 128],
                                             bufB[:, j * TT:(j + 1) * TT],
                                             start=(j == 0), stop=(j == NCH - 1))
                        return last
                    return f
                if c == 0:
                    emit("tensor", (lambda bq=bq, wq=wq: lambda tn: [
                        tn.matmul(bq[:, :], wq[:, j * 128:(j + 1) * 128],
                                  bufB[:, j * TT:(j + 1) * TT],
                                  start=(j == 0), stop=False) for j in range(8)][-1])(),
                         waits=mmw + mk_waits(f"a5h_{t}"))
                    rec(f"qmm_{t}_{c}", "sT", emit(
                        "tensor", (lambda bq=bq, wq=wq: lambda tn: [
                            tn.matmul(bq[:, :], wq[:, j * 128:(j + 1) * 128],
                                      bufB[:, j * TT:(j + 1) * TT],
                                      start=False, stop=(j == NCH - 1))
                            for j in range(8, NCH)][-1])(),
                        waits=mk_waits(f"a5_{t}"), inc="sT"))
                else:
                    rec(f"qmm_{t}_{c}", "sT", emit("tensor", _mmg(bq, wq), waits=mmw,
                                                   inc="sT"))
                rec(f"kmm_{t}_{c}", "sT", emit("tensor", _mmg(bk, wk), inc="sT"))
                rec(f"vmm_{t}_{c}", "sT", emit("tensor", _mmg(bv, wv), inc="sT"))
                if c >= 1:
                    def _rc(s, cc=c - 1):
                        s.activation(a1[:, :], d_b[:, :], AF.Ln, bias=eps9[:, 0:1])
                        return s.activation(r_b[:, :], a1[:, :], AF.Exp, scale=-1.0)
                    rec(f"r_{t}_{c-1}", "sS", emit("scalar", _rc,
                                                   waits=mk_waits(f"d_{t}_{c-1}"), inc="sS"))
                def _phiq(s, bq=bq, p=p):
                    s.activation(a1[:, :], bq[:, :], AF.Relu, scale=-1.0)
                    return s.activation(e_q[p][:, :], a1[:, :], AF.Exp, scale=-1.0)
                rec(f"eq_{t}_{c}", "sS", emit("scalar", _phiq,
                                              waits=mk_waits(f"qmm_{t}_{c}"), inc="sS"))
                def _phik(s, bk=bk, p=p):
                    s.activation(a1[:, :], bk[:, :], AF.Relu, scale=-1.0)
                    return s.activation(e_k[p][:, :], a1[:, :], AF.Exp, scale=-1.0)
                rec(f"ek_{t}_{c}", "sS", emit("scalar", _phik,
                                              waits=mk_waits(f"kmm_{t}_{c}"), inc="sS"))
                if c >= 1:
                    cc = c - 1
                    def _ap(v, cc=cc):
                        return v.scalar_tensor_tensor(
                            bufA[:, cc * TT:(cc + 1) * TT], n_b[:, :], SCALE_O, r_b[:, :],
                            OP.mult, OP.mult)
                    rec(f"ap_{t}_{cc}", "sV", emit("vector", _ap,
                                                   waits=mk_waits(f"r_{t}_{cc}"), inc="sV"))
                def _qs(v, bq=bq, p=p):
                    return v.scalar_tensor_tensor(q_s[:, :], bq[:, :], 0.0, e_q[p][:, :],
                                                  OP.max, OP.add)
                emit("vector", _qs, waits=mk_waits(f"eq_{t}_{c}"))
                def _ks(v, bk=bk, p=p):
                    return v.scalar_tensor_tensor(k_s[:, :], bk[:, :], 0.0, e_k[p][:, :],
                                                  OP.max, OP.add)
                emit("vector", _ks, waits=mk_waits(f"ek_{t}_{c}"))
                kvw = mk_waits(f"vmm_{t}_{c}")
                if t == 0 and c == 0:
                    kvw += mk_waits("init_carK")
                if t == 4 and c == 0:
                    kvw += mk_waits("creset4")
                rec(f"kv_{t}_{c}", "sV", emit(
                    "vector", (lambda bv: lambda v: v.tensor_tensor(
                        n_b[:, :], k_s[:, :], bv[:, :], OP.mult))(bv),
                    waits=kvw, inc="sV"))
                def _scans(v, c=c):
                    v.tensor_tensor_scan(KV_b[:, :], n_b[:, :], n_b[:, :],
                                         carKV[:, c:c + 1], OP.add, OP.bypass)
                    return v.tensor_tensor_scan(K_b[:, :], k_s[:, :], k_s[:, :],
                                                carK[:, c:c + 1], OP.add, OP.bypass)
                emit("vector", _scans)
                emit("vector", lambda v: v.tensor_mul(n_b[:, :], q_s[:, :], KV_b[:, :]))
                rec(f"d_{t}_{c}", "sV", emit(
                    "vector", lambda v: v.tensor_mul(d_b[:, :], q_s[:, :], K_b[:, :]),
                    inc="sV"))
                if t not in (3, 7):
                    def _csave(v, c=c):
                        v.tensor_copy(carKV[:, c:c + 1], KV_b[:, TT - 1:TT])
                        return v.tensor_copy(carK[:, c:c + 1], K_b[:, TT - 1:TT])
                    emit("vector", _csave)
            if t == 0:
                def _wbk15(g):
                    g.dma_start(cqkv_h[15], Wb[3][:, :]).then_inc(sems["dWB"], 16)
                    g.dma_start(cqkv_h[31], Wb[4][:, :]).then_inc(sems["dWB"], 16)
                    g.dma_start(cqkv_h[47], Wb[5][:, :]).then_inc(sems["dWB"], 16)
                rec("wbq_15", "dWB", emit("gpsimd", _wbk15,
                                          waits=mk_waits("vmm_0_15"), inc="dWB", k=48,
                                          self_inc=True))
                rec("cacheQ", "sG", emit("gpsimd", lambda g: g.nop(),
                                         waits=[("dWB", cnt["dWB"])], inc="sG"))
            def _rc15(s):
                s.activation(a1[:, :], d_b[:, :], AF.Ln, bias=eps9[:, 0:1])
                return s.activation(r_b[:, :], a1[:, :], AF.Exp, scale=-1.0)
            rec(f"r_{t}_15", "sS", emit("scalar", _rc15,
                                        waits=mk_waits(f"d_{t}_15"), inc="sS"))
            def _ap15(v):
                return v.scalar_tensor_tensor(
                    bufA[:, 15 * TT:16 * TT], n_b[:, :], SCALE_O, r_b[:, :],
                    OP.mult, OP.mult)
            rec(f"ap_{t}_15", "sV", emit("vector", _ap15,
                                         waits=mk_waits(f"r_{t}_15"), inc="sV"))

            if t == 1:
                def _park(g):
                    g.dma_start(DB[0][:, 0:2048], wgate_h[0]).then_inc(sems["dW"], 16)
                    g.dma_start(DB[1][:, 0:2048], wup_h[0]).then_inc(sems["dW"], 16)
                rec("wgu_1_0", "dW", emit("gpsimd", _park,
                                          waits=mk_waits("fmm_0_15"), inc="dW", k=32,
                                          self_inc=True))
                def _park2(g):
                    g.dma_start(DB[0][:, 2048:4096], wgate_h[1]).then_inc(sems["dW"], 16)
                    g.dma_start(DB[1][:, 2048:4096], wup_h[1]).then_inc(sems["dW"], 16)
                rec("wgu_1_1", "dW", emit("gpsimd", _park2, inc="dW", k=32,
                                          self_inc=True))
            for i in range(FI):
                p = i % 2
                bg, bu = pb[2 * p], pb[2 * p + 1]
                wg, wu = Wb[2 * p], Wb[2 * p + 1]
                if i == 0:
                    war = mk_waits(f"vmm_{t}_14")
                elif i == 1:
                    war = mk_waits(f"vmm_{t}_15")
                else:
                    war = mk_waits(f"umm_{t}_{i-2}")
                wgs = wg[:, :]
                wus = wu[:, :]
                if t == 1 and i < 2:
                    wgs = DB[0][:, i * 2048:(i + 1) * 2048]
                    wus = DB[1][:, i * 2048:(i + 1) * 2048]
                elif t <= 1:
                    if t == 0 and i == 0:
                        war += mk_waits("wbq_14")
                    elif t == 0 and i == 1:
                        war += mk_waits("wbq_15")
                    if t == 1 and i == 2:
                        war += mk_waits(f"vmm_{t}_14")
                    if t == 1 and i == 3:
                        war += mk_waits(f"vmm_{t}_15")
                    if t == 1 and i >= 2:
                        war += mk_waits(f"wbgu_{i-2}")
                    def _wdma(wg, wu, i):
                        def f(g):
                            g.dma_start(wg[:, :], wgate_h[i]).then_inc(sems["dW"], 16)
                            g.dma_start(wu[:, :], wup_h[i]).then_inc(sems["dW"], 16)
                        return f
                    rec(f"wgu_{t}_{i}", "dW", emit("gpsimd", _wdma(wg, wu, i),
                                                   waits=war, inc="dW", k=32,
                                                   self_inc=True))
                else:
                    def _wdma(wg, wu, i):
                        def f(g):
                            g.dma_start(wg[:, :], cgate_h[i]).then_inc(sems["dW2"], 16)
                            g.dma_start(wu[:, :], cup_h[i]).then_inc(sems["dW2"], 16)
                        return f
                    rec(f"wgu_{t}_{i}", "dW2", emit("sync", _wdma(wg, wu, i),
                                                    waits=war, inc="dW2", k=32,
                                                    self_inc=True))
                if t == 1 and i >= 1:
                    ii = i - 1
                    if ii < 2:
                        gp = DB[0][:, ii * 2048:(ii + 1) * 2048]
                        up_ = DB[1][:, ii * 2048:(ii + 1) * 2048]
                    else:
                        gp = Wb[2 * (ii % 2)][:, :]
                        up_ = Wb[2 * (ii % 2) + 1][:, :]
                    def _wbgu(gp, up_, ii):
                        def f(g):
                            g.dma_start(cgate_h[ii], gp).then_inc(sems["dWB"], 16)
                            g.dma_start(cup_h[ii], up_).then_inc(sems["dWB"], 16)
                        return f
                    rec(f"wbgu_{ii}", "dWB",
                        emit("gpsimd", _wbgu(gp, up_, ii),
                             waits=mk_waits(f"umm_{t}_{ii}"), inc="dWB", k=32,
                             self_inc=True))
                if i == 0:
                    pw = mk_waits(f"kv_{t}_14")
                elif i == 1:
                    pw = mk_waits(f"kv_{t}_15")
                else:
                    pw = mk_waits(f"hdn_{t}_{i-2}")
                def _mmg(bank, slab):
                    def f(tn):
                        last = None
                        for j in range(NCH):
                            last = tn.matmul(bank[:, :], slab[:, j * 128:(j + 1) * 128],
                                             bufB[:, j * TT:(j + 1) * TT],
                                             start=(j == 0), stop=(j == NCH - 1))
                        return last
                    return f
                rec(f"gmm_{t}_{i}", "sT", emit(
                    "tensor", _mmg(bg, wgs), waits=mk_waits(f"wgu_{t}_{i}") + pw, inc="sT"))
                rec(f"umm_{t}_{i}", "sT", emit("tensor", _mmg(bu, wus), inc="sT"))
                sw = mk_waits(f"gmm_{t}_{i}")
                if t == 0 and i == 0:
                    sw += mk_waits("xtile_0")
                def _silu(s, bg=bg, i=i, p=p):
                    return s.activation(sg_buf[p][:, :], bg[:, :], AF.Silu,
                                        scale=gs_s[:, i:i + 1])
                rec(f"silu_{t}_{i}", "sS", emit("scalar", _silu, waits=sw, inc="sS"))
                def _hdn(v, bu=bu, i=i, p=p):
                    return v.scalar_tensor_tensor(
                        hdn_t[:, i * TT:(i + 1) * TT], bu[:, :], us_s[:, i:i + 1],
                        sg_buf[p][:, :], OP.mult, OP.mult)
                rec(f"hdn_{t}_{i}", "sV", emit(
                    "vector", _hdn,
                    waits=mk_waits(f"silu_{t}_{i}", f"umm_{t}_{i}"), inc="sV"))
                if t + 1 < NTILE and i >= 4 and i % 2 == 0 and (i - 4) // 2 < NCH:
                    j = (i - 4) // 2
                    spw = mk_waits(f"pcast_{t+1}_{j}")
                    if j == 0:
                        spw += mk_waits(f"bcevac_{t}", f"kv_{t}_0")
                    def _sp(tn, j=j):
                        tn.matmul(pb[6][0:1, :], ones_col[:, 0:1],
                                  xscb4[j % 2][:, :],
                                  start=(j == 0), stop=(j == NCH - 1),
                                  skip_group_check=True)
                        return tn.matmul(pb[7][0:1, :], ones_col[:, 0:1],
                                         xscb4[2 + j % 2][:, :],
                                         start=(j == 0), stop=(j == NCH - 1),
                                         skip_group_check=True)
                    rec(f"sp_{t+1}_{j}", "sT", emit("tensor", _sp, waits=spw,
                                                    inc="sT"))
                if t + 1 < NTILE and i % 2 == 0 and i // 2 < NCH:
                    j = i // 2
                    dw = mk_waits(f"pcast_{t+1}_{j-2}") if j >= 2 else []
                    rec(f"px_{t+1}_{j}", "dP", emit(
                        "sync", (lambda j=j, t1=t + 1: lambda g: g.dma_start(
                            stg32[j % 2][:, :],
                            x_h[j * 128:(j + 1) * 128, t1 * TT:(t1 + 1) * TT]))(),
                        waits=dw, inc="dP", k=16))
                    cw = mk_waits(f"px_{t+1}_{j}")
                    if j >= 2:
                        cw += mk_waits(f"sp_{t+1}_{j-2}")
                    def _pc(s, j=j):
                        s.activation(xscb4[j % 2][:, :], stg32[j % 2][:, :], AF.Copy)
                        return s.activation(xscb4[2 + j % 2][:, :],
                                            xscb4[j % 2][:, :], AF.Square)
                    rec(f"pcast_{t+1}_{j}", "sS", emit("scalar", _pc, waits=cw,
                                                       inc="sS"))

            if t + 1 < NTILE:
                def _rowm1(v):
                    v.tensor_scalar_mul(mean_row, pb[6][0:1, :], 1.0 / D)
                    v.tensor_mul(m2_row, mean_row, mean_row)
                    return v.scalar_tensor_tensor(
                        var_row, pb[7][0:1, :], 1.0 / D, m2_row,
                        OP.mult, OP.subtract)
                rec(f"rowmath_{t+1}", "sV", emit(
                    "vector", _rowm1, waits=mk_waits(f"sp_{t+1}_{NCH-1}"), inc="sV"))
                def _rstd1(s):
                    s.activation(lnv_row, var_row, AF.Ln, bias=eps5[0:1, 0:1])
                    return s.activation(rstd_row, lnv_row, AF.Exp, scale=-0.5)
                rec(f"rstd_{t+1}", "sS", emit("scalar", _rstd1,
                                              waits=mk_waits(f"rowmath_{t+1}"),
                                              inc="sS"))
            if t == 1:
                def _wbgu43(g):
                    g.dma_start(cgate_h[43], Wb[2][:, :]).then_inc(sems["dWB"], 16)
                    g.dma_start(cup_h[43], Wb[3][:, :]).then_inc(sems["dWB"], 16)
                rec("wbgu_43", "dWB", emit("gpsimd", _wbgu43,
                                           waits=mk_waits("umm_1_43"), inc="dWB", k=32,
                                           self_inc=True))
                rec("cacheGU", "sG", emit("gpsimd", lambda g: g.nop(),
                                          waits=[("dWB", cnt["dWB"])], inc="sG"))
            if t + 1 < NTILE:
                def _bc1(tn):
                    tn.matmul(pb[6][:, :], ones_row[0:1, :], mean_row,
                              start=True, stop=True)
                    return tn.matmul(pb[7][:, :], ones_row[0:1, :], rstd_row,
                                     start=True, stop=True)
                rec(f"bcast_{t+1}", "sT", emit("tensor", _bc1,
                                               waits=mk_waits(f"rstd_{t+1}"),
                                               inc="sT"))
                def _bce1(v):
                    v.tensor_copy(mean_b[:, :], pb[6][:, :])
                    return v.tensor_copy(rstd_b[:, :], pb[7][:, :])
                rec(f"bcevac_{t+1}", "sV", emit("vector", _bce1,
                                                waits=mk_waits(f"bcast_{t+1}"),
                                                inc="sV"))
            if t + 1 < NTILE:
                for jj in (14, 15):
                    rec(f"px2_{t+1}_{jj}", "dP", emit(
                        "sync", (lambda jj=jj, t1=t + 1: lambda g: g.dma_start(
                            stg32[jj - 14][:, :],
                            x_h[jj * 128:(jj + 1) * 128, t1 * TT:(t1 + 1) * TT]))(),
                        waits=mk_waits(f"pcast_{t+1}_{jj}"), inc="dP", k=16))
            for o in range(NCH):
                p = o % 2
                ba, bf = pb[p], pb[4 + p]
                wo = Wb[4 + p]
                dbuf = DB[p]
                if o < 2:
                    war = mk_waits(f"vmm_{t}_15")
                else:
                    war = mk_waits(f"amm_{t}_{o-2}")
                if t == 0 and o < 2:
                    war += mk_waits("wbq_15")
                def _wodma(wo, o):
                    return lambda g: g.dma_start(wo[:, :], wout_h[o])
                rec(f"wo_{t}_{o}", "dW", emit("gpsimd", _wodma(wo, o), waits=war,
                                              inc="dW", k=16))
                dwar = [] if t == 0 and o < 2 else mk_waits(
                    f"fmm_{t}_{o-2}" if o >= 2 else f"fmm_{t-1}_{14+o}")
                if t == 1 and o < 2:
                    dwar += mk_waits("umm_1_1", "wbgu_1")
                def _wddma(dbuf, o):
                    return lambda g: g.dma_start(dbuf[:, :], wdown_h[o])
                rec(f"wd_{t}_{o}", "dW", emit("gpsimd", _wddma(dbuf, o), waits=dwar,
                                              inc="dW", k=16))
                mmw = mk_waits(f"wo_{t}_{o}", f"ap_{t}_15")
                if o < 2:
                    mmw += mk_waits(f"hdn_{t}_42" if o == 0 else f"hdn_{t}_43")
                else:
                    mmw += mk_waits(f"d3_{t}_{o-2}")
                def _amm(ba, wo):
                    def f(tn):
                        last = None
                        for j in range(NCH):
                            last = tn.matmul(ba[:, :], wo[:, j * 128:(j + 1) * 128],
                                             bufA[:, j * TT:(j + 1) * TT],
                                             start=(j == 0), stop=(j == NCH - 1))
                        return last
                    return f
                rec(f"amm_{t}_{o}", "sT", emit("tensor", _amm(ba, wo), waits=mmw, inc="sT"))
                fw = mk_waits(f"wd_{t}_{o}", f"hdn_{t}_43")
                def _fmm(bf, dbuf):
                    def f(tn):
                        last = None
                        for i in range(FI):
                            last = tn.matmul(bf[:, :], dbuf[:, i * 128:(i + 1) * 128],
                                             hdn_t[:, i * TT:(i + 1) * TT],
                                             start=(i == 0), stop=(i == FI - 1))
                        return last
                    return f
                rec(f"fmm_{t}_{o}", "sT", emit("tensor", _fmm(bf, dbuf), waits=fw, inc="sT"))
                if t + 1 < NTILE and o == 15:
                    rec(f"xbc_{t+1}_15", "sS", emit(
                        "scalar", lambda s: s.activation(
                            bufB[:, 15 * TT:16 * TT], stg32[1][:, :], AF.Copy),
                        waits=mk_waits(f"px2_{t+1}_15"), inc="sS"))
                    for jj in (13, 14, 15):
                        def _a5t(v, jj=jj):
                            v.tensor_sub(xscb[:, :], bufB[:, jj * TT:(jj + 1) * TT],
                                         mean_b[:, :])
                            return v.tensor_mul(bufB[:, jj * TT:(jj + 1) * TT],
                                                xscb[:, :], rstd_b[:, :])
                        val = emit("vector", _a5t,
                                   waits=mk_waits(f"xbc_{t+1}_{jj}"),
                                   inc="sV" if jj == 15 else None)
                    rec(f"a5_{t+1}", "sV", val)
                ob = ostg[o % 2]
                ocol = 0
                vw = mk_waits(f"fmm_{t}_{o}")
                sidx = 16 * t + o - 2 if o >= 2 else (16 * (t - 1) + o + 14 if t >= 1 else -1)
                if sidx >= 0:
                    vw += [("dS", 16 * (sidx + 1))]
                if t == 0 and o == 0:
                    vw += mk_waits("xtile_0")
                def _stt(v, bf=bf, o=o):
                    return v.scalar_tensor_tensor(
                        xsc[:, :], bf[:, :], ds_s[:, o:o + 1], x_t[:, o * TT:(o + 1) * TT],
                        OP.mult, OP.add)
                emit("vector", _stt, waits=vw)
                def _fin(v, ba=ba, ob=ob, ocol=ocol):
                    return v.tensor_add(ob[:, ocol:ocol + TT], xsc[:, :], ba[:, :])
                rec(f"d3_{t}_{o}", "sV", emit("vector", _fin, inc="sV"))
                if t + 1 < NTILE and o != 15:
                    if o >= 14:
                        xw = mk_waits(f"px2_{t+1}_{o}")
                        srcap = stg32[o - 14][:, :]
                    else:
                        xw = [("dX", 48 + 16 * (16 * (t + 1) + o + 1))]
                        srcap = x_t[:, o * TT:(o + 1) * TT]
                    if o == 0:
                        xw += mk_waits(f"umm_{t}_43")
                    def _xbj(s, o=o, srcap=srcap):
                        return s.activation(bufB[:, o * TT:(o + 1) * TT],
                                            srcap, AF.Copy)
                    rec(f"xbc_{t+1}_{o}", "sS", emit("scalar", _xbj, waits=xw,
                                                     inc="sS"))
                    if o >= 2:
                        jj = o - 2
                        a5w = mk_waits(f"xbc_{t+1}_{jj}")
                        if jj == 0:
                            a5w += mk_waits(f"bcevac_{t+1}")
                        def _a5j(v, jj=jj):
                            v.tensor_sub(xscb[:, :], bufB[:, jj * TT:(jj + 1) * TT],
                                         mean_b[:, :])
                            return v.tensor_mul(bufB[:, jj * TT:(jj + 1) * TT],
                                                xscb[:, :], rstd_b[:, :])
                        val = emit("vector", _a5j, waits=a5w,
                                   inc="sV" if jj in (7, NCH - 1) else None)
                        if jj == 7:
                            rec(f"a5h_{t+1}", "sV", val)
            def _emit_pre_triple(c2, t=t):
                war2 = mk_waits(f"umm_{t}_43" if c2 == 0 else f"amm_{t}_15")
                if t == 0 and c2 == 0:
                    war2 += mk_waits("wbq_15")
                if t == 1:
                    war2 += mk_waits("wbgu_43")
                wq2, wk2, wv2 = Wb[3 * c2], Wb[3 * c2 + 1], Wb[3 * c2 + 2]
                def _wdma2(wq, wk, wv, c):
                    def f(g):
                        g.dma_start(wq[:, :], cqkv_h[c]).then_inc(sems["dW2"], 16)
                        g.dma_start(wk[:, :], cqkv_h[16 + c]).then_inc(sems["dW2"], 16)
                        g.dma_start(wv[:, :], cqkv_h[32 + c]).then_inc(sems["dW2"], 16)
                    return f
                rec(f"wqkv_{t+1}_{c2}", "dW2",
                    emit("sync", _wdma2(wq2, wk2, wv2, c2), waits=war2,
                         inc="dW2", k=48, self_inc=True))
            if t + 1 < NTILE:
                _emit_pre_triple(0)
            for o in range(NCH):
                if o == 14 and t + 1 < NTILE:
                    _emit_pre_triple(1)
                if t + 1 < NTILE:
                    def _xld(g_, o=o, t1=t + 1):
                        return g_.dma_start(x_t[:, o * TT:(o + 1) * TT],
                                            x_h[o * 128:(o + 1) * 128,
                                                t1 * TT:(t1 + 1) * TT])
                    rec(f"x_{t+1}_{o}", "dX", emit("sync", _xld,
                                                   waits=mk_waits(f"d3_{t}_{o}"),
                                                   inc="dX", k=16))
                ob2 = ostg[o % 2]
                def _st(s, t=t, o=o, ob2=ob2):
                    return s.dma_start(
                        out_h[o * 128:(o + 1) * 128, t * TT:(t + 1) * TT],
                        ob2[:, :])
                rec(f"store_{t}_{o}", "dS", emit(
                    "sync", _st, waits=mk_waits(f"d3_{t}_{o}"), inc="dS", k=16))
            if t + 1 < NTILE:
                rec(f"xtile_{t+1}", "dX", cnt["dX"])

        emit("sync", lambda s: s.nop(), waits=[("dS", cnt["dS"])])

        def runner(lst):
            def f(eng):
                for waits, fn, inc, k in lst:
                    for (snm, v) in waits:
                        eng.wait_ge(sems[snm], v)
                    ins = fn(eng)
                    if inc is not None:
                        ins.then_inc(sems[inc], k)
            return f

        blk.tensor(runner(ops["tensor"]))
        blk.scalar(runner(ops["scalar"]))
        blk.vector(runner(ops["vector"]))
        blk.gpsimd(runner(ops["gpsimd"]))
        blk.sync(runner(ops["sync"]))
    return nc


def _prep_inputs(L0_attn_x, L0_attn_W_qkv, L0_attn_W_out, L0_ffn_gate_w,
                 L0_ffn_up_w, L0_ffn_down_w, L0_ffn_gate_s, L0_ffn_up_s,
                 L0_ffn_down_s):
    f32 = np.float32
    wqkv = np.ascontiguousarray(
        np.asarray(L0_attn_W_qkv, f32).reshape(48, 128, 16, 128)
        .transpose(0, 3, 2, 1).reshape(48, 128, 2048))
    wout = np.ascontiguousarray(
        np.asarray(L0_attn_W_out, f32).reshape(16, 128, 16, 128)
        .transpose(0, 3, 2, 1).reshape(16, 128, 2048))
    wgate = np.ascontiguousarray(
        np.asarray(L0_ffn_gate_w, f32).reshape(44, 128, 16, 128)
        .transpose(0, 3, 2, 1).reshape(44, 128, 2048))
    wup = np.ascontiguousarray(
        np.asarray(L0_ffn_up_w, f32).reshape(44, 128, 16, 128)
        .transpose(0, 3, 2, 1).reshape(44, 128, 2048))
    wdown = np.ascontiguousarray(
        np.asarray(L0_ffn_down_w, f32).reshape(16, 128, 44, 128)
        .transpose(0, 3, 2, 1).reshape(16, 128, 5632))
    gs = np.ascontiguousarray(np.asarray(L0_ffn_gate_s, f32).reshape(44, 128).T)
    us = np.ascontiguousarray(np.asarray(L0_ffn_up_s, f32).reshape(44, 128).T)
    ds = np.ascontiguousarray(np.asarray(L0_ffn_down_s, f32).reshape(16, 128).T)
    x = np.asarray(L0_attn_x, f32)
    in_maps = []
    for c in range(8):
        xc = np.ascontiguousarray(x[2 * c:2 * c + 2].reshape(2 * 2048, 2048).T)
        in_maps.append(dict(x=xc, wqkv=wqkv, wout=wout, wgate=wgate, wup=wup,
                            wdown=wdown, gs=gs, us=us, ds=ds))
    return in_maps


def kernel(**inputs):
    in_maps = _prep_inputs(**inputs)
    nc = build_nc()
    res = run_bass_kernel_spmd(nc, in_maps, core_ids=list(range(8)))
    out = np.empty((16, 2048, 2048), dtype=np.float32)
    for c in range(8):
        oc = np.asarray(res.results[c]["out"])
        out[2 * c:2 * c + 2] = oc.T.reshape(2, 2048, 2048)
    return out
